# version 01 (final)
# baseline (speedup 1.0000x reference)
import numpy as np
from contextlib import ExitStack

import concourse.bass as bass
import concourse.mybir as mybir
from concourse.bass_utils import run_bass_kernel_spmd
from concourse.library_overlay import lower_extended_insts
from concourse.library_config import sparse_gather as LIB_SG, mlp as LIB_MLP

F32 = mybir.dt.float32
BF16 = mybir.dt.bfloat16
U32 = mybir.dt.uint32
U16 = mybir.dt.uint16
AF = mybir.ActivationFunctionType
ALU = mybir.AluOpType
AX = mybir.AxisListType

B, S, D = 4, 8192, 1024
TC = 4096
H, FM, HD = 16, 64, 64
NLAY, ED, SN, CK, DR = 2, 2048, 16, 4, 64
EH = ED // 2
NE, FH = 8, 2048
CAP = 768
W16 = CAP // 16
EPS = 1e-5
TRASH = 4096
import os as _os
STAGES = 1
A_PHASE = 7
A_SUB = int(_os.environ.get('A_SUB', '9'))

DC = D // 128
LG = TC // 128

CONST_VALS = [-1.0, float(TRASH), -float(TRASH), 1.0 / D, 8128.0, EPS, 1.0, 16.0]


def _atlas_spec():
    ent = [("wqkvT_qk", (D, 2048)), ("wqkvT_v", (D, D)), ("woutT", (D, D)),
           ("gate_wT", (D, NE))]
    for e in range(NE):
        ent += [(f"gate_bank{e}", (D, FH)), (f"up_bank{e}", (D, FH)),
                (f"down_bank{e}", (FH, D))]
    for i in range(NLAY):
        for hf in range(2):
            ent += [(f"in_projT{i}_{hf}", (D, 2 * EH)),
                    (f"xp_wT{i}_{hf}", (EH, DR + 2 * SN)),
                    (f"dt_wT{i}_{hf}", (DR, EH)),
                    (f"out_wT{i}_{hf}", (EH, D))]
    off, offs = 0, {}
    for name, shp in ent:
        offs[name] = (off, shp)
        off += int(np.prod(shp))
    return offs, off


ATLAS, ATLAS_N = _atlas_spec()
ATLAS_HALF = (ATLAS_N + 1) // 2


_VEC_OPS = {"tensor_tensor", "scalar_tensor_tensor", "tensor_scalar",
            "tensor_copy", "reduce_sum", "reduce_max", "tensor_reduce",
            "memset", "memzero", "reciprocal", "select", "tensor_tensor_scan",
            "tensor_add", "tensor_sub", "tensor_mul", "tensor_max",
            "tensor_scalar_add", "tensor_scalar_mul", "tensor_scalar_max",
            "activation", "copy", "add", "mul", "sqrt", "square", "sign",
            "iota", "sparse_gather", "indirect_copy", "transpose"}


class SafeEng:

    def __init__(self, eng, em, chain):
        self._e = eng
        self.em = em
        self.chain = chain
        em.sem(chain)

    def __getattr__(self, name):
        a = getattr(self._e, name)
        if name in _VEC_OPS:
            def wrapped(*args, **kw):
                k = self.em.cnt[self.chain]
                if k > 0:
                    self.em.wait(self._e, self.chain, k)
                ins = a(*args, **kw)
                self.em.inc(ins, self.chain, 1)
                return ins
            return wrapped
        return a

    def mark(self, name):
        self.em.marks[name] = (self.chain, self.em.cnt[self.chain])


class Emit:
    def __init__(self, nc, es):
        self.nc = nc
        self.es = es
        self._sems = {}
        self.cnt = {}
        self.marks = {}
        self._waited = {}

    def sem(self, name):
        if name not in self._sems:
            self._sems[name] = self.es.enter_context(self.nc.semaphore(name))
            self.cnt[name] = 0
        return self._sems[name]

    def inc(self, instr, name, k):
        instr.then_inc(self.sem(name), k)
        self.cnt[name] += k
        return self.cnt[name]

    def setcnt(self, name, v):
        self.sem(name)
        self.cnt[name] = v

    def wait(self, eng, name, value=None):
        if name in self.marks and value is None:
            name, value = self.marks[name]
        self.sem(name)
        v = self.cnt[name] if value is None else value
        if v <= 0:
            return
        key = (eng.engine, name)
        if self._waited.get(key, -1) >= v:
            return
        eng.wait_ge(self._sems[name], v)
        self._waited[key] = v


def _ln_pass(nc, t, em, sb, tag):
    v, sc, sy = sb["V"], sb["SC"], nc.sync
    st8 = sb["lnst"]
    ntok, atok = [], []
    for g in range(LG):
        pi = g % 2
        xch = sb["xch"][:, pi * D:(pi + 1) * D]
        lnb = sb["lnb"][:, pi * D:(pi + 1) * D]
        if g >= 2:
            em.wait(sy, "vch", ntok[g - 2])
        ins = sy.dma_start(out=xch, in_=t["x_own"][g * 128:(g + 1) * 128, :])
        em.inc(ins, tag + "_ld", 16)
        em.wait(v._e, tag + "_ld", 16 * (g + 1))
        if g >= 1:
            em.wait(v._e, "sch", atok[g - 1])
        v.scalar_tensor_tensor(sb["sq"][:], xch, 1.0, xch, ALU.mult, ALU.mult,
                               accum_out=st8[:, 1:2])
        v.reduce_sum(st8[:, 0:1], xch, axis=AX.X)
        v.mark(tag + "_s")
        em.wait(sc._e, tag + "_s")
        sc.activation(st8[:, 2:3], st8[:, 0:1], AF.Identity, scale=1.0 / D)
        sc.activation(st8[:, 3:4], st8[:, 1:2], AF.Identity, scale=1.0 / D)
        sc.mark(tag + "_m")
        em.wait(v._e, tag + "_m")
        v.tensor_tensor(st8[:, 4:5], st8[:, 2:3], st8[:, 2:3], ALU.mult)
        v.scalar_tensor_tensor(st8[:, 5:6], st8[:, 4:5], -1.0, st8[:, 3:4],
                               ALU.mult, ALU.add)
        v.mark(tag + "_v")
        em.wait(sc._e, tag + "_v")
        sc.activation(st8[:, 6:7], st8[:, 5:6], AF.Ln, bias=EPS)
        sc.activation(st8[:, 6:7], st8[:, 6:7], AF.Exp, scale=-0.5)
        sc.mark(tag + "_r")
        em.wait(v._e, tag + "_r")
        v.tensor_tensor(st8[:, 7:8], st8[:, 2:3], st8[:, 6:7], ALU.mult)
        v.mark(tag + "_q")
        em.wait(sc._e, tag + "_q")
        sc.activation(st8[:, 7:8], st8[:, 7:8], AF.Identity, scale=-1.0)
        sc.mark(tag + "_p")
        atok.append(em.cnt["sch"])
        em.wait(v._e, tag + "_p")
        if g >= 2:
            em.wait(v._e, tag + "_w")
        v.tensor_scalar(lnb, xch, st8[:, 6:7], st8[:, 7:8], ALU.mult, ALU.add)
        v.mark(tag + "_n")
        ntok.append(em.cnt["vch"])
        em.wait(sy, tag + "_n")
        ins = sy.dma_start(out=t["ln_tm"][g * 128:(g + 1) * 128, :], in_=lnb)
        em.inc(ins, tag + "_w", 16)


def _fm_read(nc, t, em, sb, tag):
    sy = nc.sync
    em.wait(sy, tag + "_w")
    for dc in range(DC):
        ins = sy.dma_start(out=sb["lnfm"][:, dc * TC:(dc + 1) * TC],
                           in_=t["ln_tm"][0:TC, dc * 128:(dc + 1) * 128],
                           transpose=True)
        em.inc(ins, tag + "_fm", 16)


def _stage_fence(nc, em):
    for eng in (nc.vector, nc.scalar, nc.tensor, nc.gpsimd, nc.sync):
        em.wait(eng, "xstage")


def _barrier(nc, t, em):
    gp = nc.gpsimd
    ins = gp.collective_compute(
        "AllReduce", ALU.add, replica_groups=[list(range(8))],
        ins=[t["cc_in"][:, :].opt()], outs=[t["cc_out"][:, :].opt()])
    em.inc(ins, "bar", 1)


def _attn_a(nc, t, em, sb, atl, alloc):
    v, sc = sb["V"], sb["SC"]
    te, gp, sy = nc.tensor, nc.gpsimd, nc.sync
    em.wait(sy, "xinit")
    _ln_pass(nc, t, em, sb, "aln")
    _fm_read(nc, t, em, sb, "aln")

    ftq = alloc("a_ftq", [128, TC], F32)
    wflt = alloc("a_wflt", [128, 3 * D], F32)
    rows = alloc("a_rows", [1, 4 * D + 128], F32)
    rows_bf = alloc("a_rowsb", [1, 4 * D + 128], BF16)
    selri = alloc("a_selri", [128, FM], F32)
    sel64 = alloc("a_sel64", [64, 128], F32)
    wbuf = alloc("a_wbuf", [128, 16384], BF16)
    ftq_bf = wbuf[:, 8192:12288]
    f2is = wbuf[:, 12288:16384]
    qkvtm = alloc("a_qkvtm", [128, 2048], F32)
    qkvtm_bf = alloc("a_qkvtmb", [128, 2048], BF16)
    qf = alloc("a_qf", [128, 3 * D], F32)
    big = alloc("a_big", [128, 3 * D], F32)
    qfp = big[:, 0:3 * D]
    hp = big[:, 0:2 * D]
    sml = alloc("a_sml", [128, 192], F32)
    voutf = alloc("a_voutf", [128, D], BF16)
    xres = qkvtm[:, 0:2 * D]
    FRS0, BQ0, BO0 = 0, 128, 128 + 3 * D

    ins = sy.dma_start(out=ftq[:].rearrange("p (g f) -> p g f", f=128),
                       in_=t["ftq"][:, :].rearrange("(g p) f -> p g f", p=128))
    em.inc(ins, "a_tb", 16)
    for nm, dst in (("wfilt", wflt[:]), ("frs", rows[0:1, FRS0:FRS0 + 128]),
                    ("bqkv", rows[0:1, BQ0:BQ0 + 3 * D]),
                    ("bout", rows[0:1, BO0:BO0 + D]),
                    ("selri", selri[:]), ("sel64", sel64[0:64, :])):
        ins = sy.dma_start(out=dst, in_=t[nm][:, :])
        em.inc(ins, "a_tb", 16)
    em.wait(v._e, "a_tb")
    v.tensor_copy(rows_bf[0:1, :], rows[0:1, :])
    v.mark("a_rbf")

    em.wait(sy, "bar", 1)
    wqk = atl("wqkvT_qk")
    for dc in range(DC):
        ins = sy.dma_start(out=wbuf[:, dc * 2048:(dc + 1) * 2048],
                           in_=wqk[dc * 128:(dc + 1) * 128, :])
        em.inc(ins, "a_wqk", 16)

    pq = [sb["pq0"], sb["pq1"]]
    pdft = sb["pdft"]
    em.wait(te, "aln_fm")
    em.wait(te, "a_wqk")
    em.wait(te, "a_tb")

    qev, dmm, qmm = [], [], []
    for qh in range(2):
        for g in range(LG):
            pi = g % 2
            idx = qh * LG + g
            if idx >= 2:
                em.wait(te, "vch", qev[idx - 2])
            for dc in range(DC):
                lhs = sb["lnfm"][:, dc * TC + g * 128: dc * TC + g * 128 + 128]
                mov = wbuf[:, dc * 2048 + qh * 1024: dc * 2048 + (qh + 1) * 1024]
                te.matmul(pq[pi][:, 0:512], lhs, mov[:, 0:512],
                          start=(dc == 0), stop=(dc == DC - 1))
                ins = te.matmul(pq[pi][:, 512:1024], lhs, mov[:, 512:1024],
                                start=(dc == 0), stop=(dc == DC - 1))
            em.inc(ins, "a_qmm", 1)
            qmm.append(em.cnt["a_qmm"])
            em.wait(v._e, "a_qmm", qmm[idx])
            if idx >= 2:
                em.wait(v._e, "a_dmm", idx - 1)
            v.tensor_copy(qkvtm[:, pi * 1024:(pi + 1) * 1024], pq[pi][:, :])
            qev.append(em.cnt["vch"])
            em.wait(te, "vch", qev[idx])
            dst = pdft[:, qh * 1024:(qh + 1) * 1024]
            lhs = ftq[:, g * 128:(g + 1) * 128]
            mv = qkvtm[:, pi * 1024:(pi + 1) * 1024]
            te.matmul(dst[:, 0:512], lhs, mv[:, 0:512], start=(g == 0), stop=False)
            ins = te.matmul(dst[:, 512:1024], lhs, mv[:, 512:1024],
                            start=(g == 0), stop=False)
            em.inc(ins, "a_dmm", 1)
        te.matmul(pdft[:, qh * 1024:qh * 1024 + 512], rows[0:1, FRS0:FRS0 + 128],
                  rows[0:1, BQ0 + qh * 1024:BQ0 + qh * 1024 + 512],
                  start=False, stop=True)
        ins = te.matmul(pdft[:, qh * 1024 + 512:(qh + 1) * 1024],
                        rows[0:1, FRS0:FRS0 + 128],
                        rows[0:1, BQ0 + qh * 1024 + 512:BQ0 + (qh + 1) * 1024],
                        start=False, stop=True)
        em.inc(ins, "a_dftqk", 1)
    em.wait(v._e, "a_dftqk", 2)
    v.tensor_copy(qf[:, 0:1024], pdft[:, 0:1024])
    v.tensor_copy(qf[:, 1024:2048], pdft[:, 1024:2048])
    v.mark("a_qkev")

    wv = atl("wqkvT_v")
    em.wait(sy, "a_qmm", 2 * LG)
    for dc in range(DC):
        ins = sy.dma_start(out=wbuf[:, dc * 1024:(dc + 1) * 1024],
                           in_=wv[dc * 128:(dc + 1) * 128, :])
        em.inc(ins, "a_wv", 16)
    em.wait(v._e, "a_qmm", 2 * LG)
    v.tensor_copy(ftq_bf[:], ftq[:])
    v.mark("a_cst")
    em.wait(te, "a_wv")
    em.wait(te, "a_qkev")
    em.wait(te, "a_cst")
    vev, vdm = [], []
    for g in range(LG):
        pi = g % 2
        if g >= 2:
            em.wait(te, "vch", vev[g - 2])
        for dc in range(DC):
            lhs = sb["lnfm"][:, dc * TC + g * 128: dc * TC + g * 128 + 128]
            mov = wbuf[:, dc * 1024:(dc + 1) * 1024]
            te.matmul(pq[pi][:, 0:512], lhs, mov[:, 0:512],
                      start=(dc == 0), stop=(dc == DC - 1))
            ins = te.matmul(pq[pi][:, 512:1024], lhs, mov[:, 512:1024],
                            start=(dc == 0), stop=(dc == DC - 1))
        em.inc(ins, "a_vmm", 1)
        em.wait(v._e, "a_vmm", g + 1)
        if g >= 2:
            em.wait(v._e, "a_vdm", g - 1)
        v.tensor_copy(qkvtm_bf[:, pi * 1024:(pi + 1) * 1024], pq[pi][:, :])
        vev.append(em.cnt["vch"])
        em.wait(te, "vch", vev[g])
        te.matmul(pdft[:, 0:512], ftq_bf[:, g * 128:(g + 1) * 128],
                  qkvtm_bf[:, pi * 1024:pi * 1024 + 512],
                  start=(g == 0), stop=False)
        ins = te.matmul(pdft[:, 512:1024], ftq_bf[:, g * 128:(g + 1) * 128],
                        qkvtm_bf[:, pi * 1024 + 512:(pi + 1) * 1024],
                        start=(g == 0), stop=False)
        em.inc(ins, "a_vdm", 1)
    te.matmul(pdft[:, 0:512], rows_bf[0:1, FRS0:FRS0 + 128],
              rows_bf[0:1, BQ0 + 2048:BQ0 + 2560], start=False, stop=True)
    ins = te.matmul(pdft[:, 512:1024], rows_bf[0:1, FRS0:FRS0 + 128],
                    rows_bf[0:1, BQ0 + 2560:BQ0 + 3072], start=False, stop=True)
    em.inc(ins, "a_dftv", 1)
    em.wait(v._e, "a_dftv", 1)
    v.tensor_copy(qf[:, 2048:3072], pdft[:, 0:1024])
    v.mark("a_vfev")

    em.wait(sy, "a_qkev")
    em.wait(sy, "a_vfev")
    psh = t["p_sh"].rearrange("s (p f) -> s p f", p=128)
    for arm in nc.switch_core_id(engines=[sy], n=128):
        half = arm.logical % 2
        sy.dma_start(out=psh[half, :, 0:3 * D], in_=qf[:]).then_inc(
            em.sem("a_pw"), 16)
    em.setcnt("a_pw", 16)
    em.wait(gp, "a_pw", 16)
    _barrier(nc, t, em)
    em.wait(sy, "bar")
    for arm in nc.switch_core_id(engines=[sy], n=128):
        half = arm.logical % 2
        sy.dma_start(out=qfp[:], in_=psh[1 - half, :, 0:3 * D]).then_inc(
            em.sem("a_pr"), 16)
    em.setcnt("a_pr", 16)
    em.wait(v._e, "a_pr", 16)
    v.tensor_tensor(qf[:], qf[:], qfp[:], ALU.add)

    v.tensor_tensor(hp[:, 0:1024], qf[:, 0:1024], wflt[:, 0:1024], ALU.mult)
    v.tensor_tensor(hp[:, 1024:2048], qf[:, 1024:2048], wflt[:, 1024:2048],
                    ALU.mult)
    v.tensor_tensor(hp[:, 0:1024], hp[:, 0:1024], hp[:, 1024:2048], ALU.mult)
    v.reduce_sum(sml[:, 0:16],
                 hp[:, 0:1024].rearrange("p (h w) -> p h w", w=HD), axis=AX.X)
    v.tensor_tensor(hp[:, 1024:2048], qf[:, 2048:3072], wflt[:, 2048:3072],
                    ALU.mult)
    v.mark("a_hw")
    em.wait(te, "a_hw")
    ins = te.matmul(pq[0][0:16, 0:64], sml[:, 0:16], selri[:, :],
                    start=True, stop=True)
    em.inc(ins, "a_sc", 1)
    em.wait(v._e, "a_sc", 1)
    v.reduce_max(sml[0:16, 16:17], pq[0][0:16, 0:64], axis=AX.X)
    v.tensor_scalar(sml[0:16, 17:18], sml[0:16, 16:17], 0.0, None, ALU.max)
    v.mark("a_mx")
    em.wait(sc._e, "a_mx")
    sc.activation(sml[0:16, 18:19], sml[0:16, 17:18], AF.Identity, scale=-1.0)
    sc.activation(sml[0:16, 64:128], pq[0][0:16, 0:64], AF.Exp,
                  bias=sml[0:16, 18:19])
    sc.activation(sml[0:16, 19:20], sml[0:16, 17:18], AF.Exp, scale=-1.0)
    sc.mark("a_exp")
    em.wait(v._e, "a_exp")
    v.reduce_sum(sml[0:16, 20:21], sml[0:16, 64:128], axis=AX.X)
    v.scalar_tensor_tensor(sml[0:16, 21:22], sml[0:16, 19:20], 8128.0,
                           sml[0:16, 20:21], ALU.mult, ALU.add)
    v.reciprocal(sml[0:16, 22:23], sml[0:16, 21:22])
    v.tensor_scalar(sml[0:16, 64:128], sml[0:16, 64:128],
                    sml[0:16, 22:23], None, ALU.mult)
    v.mark("a_at")
    em.wait(te, "a_at")
    ins = te.transpose(pq[1][0:64, 0:16], sml[0:16, 64:128],
                       sb["ident"][0:16, 0:16])
    em.inc(ins, "a_tr", 1)
    em.wait(v._e, "a_tr", 1)
    v.tensor_copy(sml[0:64, 144:160], pq[1][0:64, 0:16])
    v.mark("a_trv")
    em.wait(te, "a_trv")
    ins = te.matmul(pq[0][:, 32:48], sel64[0:64, :], sml[0:64, 144:160],
                    start=True, stop=True)
    em.inc(ins, "a_du", 1)
    em.wait(v._e, "a_du", 1)
    v.tensor_copy(sml[:, 128:144], pq[0][:, 32:48])
    for h in range(H):
        v.tensor_scalar(voutf[:, h * HD:(h + 1) * HD],
                        hp[:, 1024 + h * HD:1024 + (h + 1) * HD],
                        sml[:, 128 + h:129 + h], None, ALU.mult)
    v.mark("a_of")

    ins = sy.dma_start(out=f2is[:], in_=t["f2i"][:, :])
    em.inc(ins, "a_f2", 16)
    em.wait(te, "a_f2", 16)
    em.wait(te, "a_of")
    iev = []
    for dc in range(DC):
        for lq in range(TC // 512):
            pp = pq[lq % 2]
            ii = dc * (TC // 512) + lq
            if ii >= 2:
                em.wait(te, "vch", iev[ii - 2])
            ins = te.matmul(pp[:, 0:512], voutf[:, dc * 128:(dc + 1) * 128],
                            f2is[:, lq * 512:(lq + 1) * 512],
                            start=True, stop=True)
            em.inc(ins, "a_imm", 1)
            em.wait(v._e, "a_imm", ii + 1)
            v.tensor_copy(
                sb["lnfm"][:, dc * TC + lq * 512: dc * TC + (lq + 1) * 512],
                pp[:, 0:512])
            iev.append(em.cnt["vch"])
    v.mark("a_iev")

    wo = atl("woutT")
    em.wait(sy, "a_vmm", LG)
    for dc in range(DC):
        ins = sy.dma_start(out=wbuf[:, dc * 1024:(dc + 1) * 1024],
                           in_=wo[dc * 128:(dc + 1) * 128, :])
        em.inc(ins, "a_wo", 16)
    em.wait(te, "a_wo")
    em.wait(te, "a_iev")
    rtok = []
    for g in range(LG):
        pi = g % 2
        if g >= 2:
            em.wait(sy, "vch", rtok[g - 2])
        ins = sy.dma_start(out=sb["xch"][:, pi * D:(pi + 1) * D],
                           in_=t["x_own"][g * 128:(g + 1) * 128, :])
        em.inc(ins, "a_xl", 16)
        if g >= 2:
            em.wait(te, "vch", rtok[g - 2])
        for dc in range(DC):
            lhs = sb["lnfm"][:, dc * TC + g * 128: dc * TC + g * 128 + 128]
            mov = wbuf[:, dc * 1024:(dc + 1) * 1024]
            te.matmul(pq[pi][:, 0:512], lhs, mov[:, 0:512],
                      start=(dc == 0), stop=False)
            te.matmul(pq[pi][:, 512:1024], lhs, mov[:, 512:1024],
                      start=(dc == 0), stop=False)
        te.matmul(pq[pi][:, 0:512], sb["ones_bf"][0:1, 0:128],
                  rows_bf[0:1, BO0:BO0 + 512], start=False, stop=True)
        ins = te.matmul(pq[pi][:, 512:1024], sb["ones_bf"][0:1, 0:128],
                        rows_bf[0:1, BO0 + 512:BO0 + 1024], start=False, stop=True)
        em.inc(ins, "a_omm", 1)
        em.wait(v._e, "a_omm", g + 1)
        em.wait(v._e, "a_xl", 16 * (g + 1))
        if g >= 2:
            em.wait(v._e, "a_res")
        ins = v.tensor_tensor(xres[:, pi * D:(pi + 1) * D], pq[pi][:, :],
                              sb["xch"][:, pi * D:(pi + 1) * D], ALU.add)
        rtok.append(em.cnt["vch"])
        em.wait(sy, "vch", rtok[g])
        ins = sy.dma_start(out=t["x_own"][g * 128:(g + 1) * 128, :],
                           in_=xres[:, pi * D:(pi + 1) * D])
        em.inc(ins, "a_res", 16)
    em.wait(sy, "a_res")
    ins = sy.nop()
    em.inc(ins, "xstage", 1)


def _ffn(nc, t, em, sb, atl, alloc, tag, write_out):
    raise NotImplementedError


def _attn_b(nc, t, em, sb, atl, alloc):
    raise NotImplementedError


def _emit(nc, t, em, sb, mkalloc):
    gp, sy = nc.gpsimd, nc.sync

    def atl(name):
        off, shp = ATLAS[name]
        return t["wsh"][0, off:off + int(np.prod(shp))].rearrange(
            "(a b) -> a b", b=shp[1])

    ins = sy.dma_start(out=sb["ident"][:], in_=t["ident"][:, :])
    em.inc(ins, "ld0", 16)
    ins = sy.dma_start(out=sb["consts"][:], in_=t["consts"][:, :])
    em.inc(ins, "ld0", 16)
    ins = sy.dma_start(out=sb["ltri"][:], in_=t["ltri"][:, :])
    em.inc(ins, "ld0", 16)
    ins = sy.dma_start(out=sb["iotap1"][:], in_=t["iotap1"][:, :])
    em.inc(ins, "ld0", 16)
    ins = sy.dma_start(out=sb["posc"][:], in_=t["posc"][:, :])
    em.inc(ins, "ld0", 16)
    ins = sy.dma_start(out=sb["iota8"][:], in_=t["iota8"][:, :])
    em.inc(ins, "ld0", 16)
    ins = sy.dma_start(out=sb["ones_bf"][:], in_=t["ones128"][:, :])
    em.inc(ins, "ld0", 16)
    for ci, val in enumerate(CONST_VALS):
        nc.const_aps.register(F32, float(val), sb["consts"][:, ci:ci + 1])
    for eng in (nc.vector, nc.scalar, nc.tensor):
        em.wait(eng, "ld0")

    em.wait(sy, "ld0")
    ins = sy.dma_start(out=t["cc_in"][:, :], in_=sb["consts"][0:1, 0:8])
    em.inc(ins, "ccinw", 16)
    em.wait(gp, "ccinw", 16)

    for arm in nc.switch_core_id(engines=[gp], n=128):
        half = arm.logical % 2
        base = half * ATLAS_HALF
        n0 = min(ATLAS_HALF, ATLAS_N - base)
        gp.dma_start(out=t["wsh"][0, base:base + n0],
                     in_=t["wsrc"][0, 0:n0]).then_inc(em.sem("wconv"), 16)
    em.setcnt("wconv", 16)
    em.wait(gp, "wconv", 16)
    _barrier(nc, t, em)

    ins = sy.dma_start(out=t["x_own"][:, :], in_=t["xq"][:, :])
    em.inc(ins, "xinit", 16)
    em.setcnt("xstage", 0)

    if STAGES >= 1:
        with ExitStack() as st_es:
            _attn_a(nc, t, em, sb, atl, mkalloc(st_es))
    if STAGES >= 2:
        with ExitStack() as st_es:
            _stage_fence(nc, em)
            _ffn(nc, t, em, sb, atl, mkalloc(st_es), "f1", False)
    if STAGES >= 3:
        with ExitStack() as st_es:
            _stage_fence(nc, em)
            _attn_b(nc, t, em, sb, atl, mkalloc(st_es))
    if STAGES >= 4:
        with ExitStack() as st_es:
            _stage_fence(nc, em)
            _ffn(nc, t, em, sb, atl, mkalloc(st_es), "f2", True)
    else:
        em.wait(sy, "xstage")
        ins = sy.dma_start(out=t["out"][:, :], in_=t["x_own"][:, :])
        em.inc(ins, "outw", 16)
    em.wait(gp, "outw")


def _build():
    nc = bass.Bass("TRN2", target_bir_lowering=False)
    t = {}

    def dram(name, shape, dtype, **kw):
        t[name] = nc.dram_tensor(name, shape, dtype, **kw).ap()

    dram("xq", [TC, D], F32, kind="ExternalInput")
    dram("wsrc", [1, ATLAS_HALF], F32, kind="ExternalInput")
    dram("ftq", [TC, 128], F32, kind="ExternalInput")
    dram("f2i", [128, TC], BF16, kind="ExternalInput")
    dram("frs", [1, 128], F32, kind="ExternalInput")
    dram("bqkv", [1, 3 * D], F32, kind="ExternalInput")
    dram("bout", [1, D], F32, kind="ExternalInput")
    dram("wfilt", [128, 3 * D], F32, kind="ExternalInput")
    dram("selri", [128, FM], F32, kind="ExternalInput")
    dram("sel64", [64, 128], F32, kind="ExternalInput")
    dram("ident", [128, 128], F32, kind="ExternalInput")
    dram("iotap1", [128, 32], F32, kind="ExternalInput")
    dram("posc", [16, W16], F32, kind="ExternalInput")
    dram("iota8", [128, NE], F32, kind="ExternalInput")
    dram("ltri", [128, 128], BF16, kind="ExternalInput")
    dram("ones128", [128, 128], BF16, kind="ExternalInput")
    dram("consts", [128, len(CONST_VALS)], F32, kind="ExternalInput")
    dram("conv_w", [NLAY, EH, CK], F32, kind="ExternalInput")
    dram("conv_b", [NLAY, 1, EH], F32, kind="ExternalInput")
    dram("dt_b", [NLAY, 1, EH], F32, kind="ExternalInput")
    dram("a_log", [NLAY, EH, SN], F32, kind="ExternalInput")
    dram("d_p", [NLAY, 1, EH], F32, kind="ExternalInput")
    dram("norm_w", [NLAY, 1, D], F32, kind="ExternalInput")

    dram("out", [TC, D], F32, kind="ExternalOutput")

    dram("x_own", [TC, D], F32)
    dram("ln_tm", [TC + 1, D], BF16)
    dram("xg_tm", [NE * CAP, D], BF16)
    dram("y_tm", [NE * CAP, D], F32)
    dram("u_sp", [EH, S], F32)
    dram("z_sp", [EH, S], F32)
    dram("yg_sp", [EH, S], BF16)
    dram("xn_tm", [S, D], BF16)
    dram("op_sp", [S, D], F32)
    dram("cc_in", [1, 8], F32)
    dram("cc_out", [1, 8], F32)

    dram("wsh", [1, ATLAS_N], BF16, addr_space="Shared")
    dram("h_sh", [S, D], F32, addr_space="Shared")
    dram("p_sh", [2, S * D // 128 * 128], F32, addr_space="Shared")

    with ExitStack() as es:
        em = Emit(nc, es)
        sb = {}

        def galloc(name, shape, dtype=F32):
            sb[name] = es.enter_context(nc.sbuf_tensor("sb_" + name, shape, dtype))
            return sb[name]

        galloc("ident", [128, 128])
        galloc("consts", [128, len(CONST_VALS)])
        galloc("ltri", [128, 128], BF16)
        galloc("iotap1", [128, 32])
        galloc("posc", [16, W16])
        galloc("iota8", [128, NE])
        galloc("ones_bf", [128, 128], BF16)
        galloc("lnfm", [128, DC * TC], BF16)
        galloc("xch", [128, 2 * D])
        galloc("lnb", [128, 2 * D], BF16)
        galloc("sq", [128, D])
        galloc("lnst", [128, 8])
        sb["V"] = SafeEng(nc.vector, em, "vch")
        sb["SC"] = SafeEng(nc.scalar, em, "sch")
        sb["GP"] = SafeEng(nc.gpsimd, em, "gch")
        sb["pq0"] = es.enter_context(nc.psum_tensor("pq0", [128, 1024], F32))
        sb["pq1"] = es.enter_context(nc.psum_tensor("pq1", [128, 1024], F32))
        sb["pdft"] = es.enter_context(nc.psum_tensor("pdft", [128, 2048], F32))

        def mkalloc(st_es):
            def alloc(name, shape, dtype=F32):
                h = st_es.enter_context(nc.sbuf_tensor(name, shape, dtype))
                return h
            return alloc

        _emit(nc, t, em, sb, mkalloc)
    lower_extended_insts(nc)
    return nc


def _bf16(a):
    import ml_dtypes
    return np.ascontiguousarray(np.asarray(a)).astype(ml_dtypes.bfloat16)


def _host_in_maps(inputs):
    ii = {k: np.ascontiguousarray(np.asarray(v)) for k, v in inputs.items()}
    x0 = ii["L0_attn_a_query"].astype(np.float32, copy=False)

    blob = np.zeros(ATLAS_N, np.float32)

    def put(name, arr):
        off, shp = ATLAS[name]
        a = np.ascontiguousarray(arr).astype(np.float32, copy=False)
        assert a.shape == shp, (name, a.shape, shp)
        blob[off:off + a.size] = a.reshape(-1)

    wqkv = ii["L0_attn_a_w_qkv"]
    put("wqkvT_qk", wqkv[:2048].T)
    put("wqkvT_v", wqkv[2048:].T)
    put("woutT", ii["L0_attn_a_w_out"].T)
    put("gate_wT", ii["L0_ffn_gate_w"].T)
    for e in range(NE):
        put(f"gate_bank{e}", ii["L0_ffn_gate_bank"][e])
        put(f"up_bank{e}", ii["L0_ffn_up_bank"][e])
        put(f"down_bank{e}", ii["L0_ffn_down_bank"][e])
    inw = ii["L0_attn_b_in_proj_w"]
    xpw = ii["L0_attn_b_x_proj_w"]
    dtw = ii["L0_attn_b_dt_w"]
    outw = ii["L0_attn_b_out_proj_w"]
    for i in range(NLAY):
        for hf in range(2):
            lo, hi = hf * EH, (hf + 1) * EH
            put(f"in_projT{i}_{hf}",
                np.concatenate([inw[i][lo:hi], inw[i][ED + lo:ED + hi]], 0).T)
            put(f"xp_wT{i}_{hf}", xpw[i].T[lo:hi])
            put(f"dt_wT{i}_{hf}", dtw[i].T[:, lo:hi])
            put(f"out_wT{i}_{hf}", outw[i].T[lo:hi])

    f_idx = np.arange(FM)
    tabs = {}
    for hf in range(2):
        s_idx = np.arange(hf * TC, (hf + 1) * TC)
        th = 2.0 * np.pi * np.outer(s_idx, f_idx) / S
        ftq = np.concatenate([np.cos(th), np.sin(th)], 1).astype(np.float32)
        f2i = _bf16(np.concatenate([np.cos(th.T), np.sin(th.T)], 0))
        frs = ftq.sum(0, keepdims=True).astype(np.float32)
        tabs[hf] = (np.ascontiguousarray(ftq), np.ascontiguousarray(f2i), frs)

    wf = np.concatenate(
        [ii[f"L0_attn_a_freq_{k}"].transpose(1, 0, 2).reshape(FM, D)
         for k in ("q", "k", "v")], 1)
    wfilt = np.ascontiguousarray(np.concatenate([wf, wf], 0)).astype(np.float32)

    selri = np.eye(FM, dtype=np.float32)[np.arange(128) % FM]
    sel64 = np.ascontiguousarray(np.eye(FM, dtype=np.float32)[:, np.arange(128) % FM])
    ident = np.eye(128, dtype=np.float32)
    iotap1 = np.zeros((128, 32), np.float32)
    for q in range(128):
        for slab in range(2):
            for r in range(16):
                iotap1[q, slab * 16 + r] = 2048 * slab + 16 * q + r + 1
    posc = (np.arange(16, dtype=np.float32)[:, None]
            + 16 * np.arange(W16, dtype=np.float32)[None, :])
    iota8 = np.broadcast_to(np.arange(NE, dtype=np.float32), (128, NE)).copy()
    ltri = _bf16(np.tril(np.ones((128, 128), np.float32), -1).T)
    ones128 = _bf16(np.ones((128, 128), np.float32))
    consts = np.broadcast_to(np.array(CONST_VALS, np.float32),
                             (128, len(CONST_VALS))).copy()
    bqkv = ii["L0_attn_a_b_qkv"].reshape(1, -1).astype(np.float32)
    bout = ii["L0_attn_a_b_out"].reshape(1, -1).astype(np.float32)

    maps = []
    for c in range(8):
        b, hf = divmod(c, 2)
        lo, hi = hf * EH, (hf + 1) * EH
        base = hf * ATLAS_HALF
        n0 = min(ATLAS_HALF, ATLAS_N - base)
        wsrc = np.zeros((1, ATLAS_HALF), np.float32)
        wsrc[0, :n0] = blob[base:base + n0]
        ftq, f2i, frs = tabs[hf]
        m = dict(
            xq=np.ascontiguousarray(x0[b, hf * TC:(hf + 1) * TC]),
            wsrc=wsrc, ftq=ftq, f2i=f2i, frs=frs,
            bqkv=bqkv, bout=bout, wfilt=wfilt, selri=selri, sel64=sel64,
            ident=ident, iotap1=iotap1, posc=posc, iota8=iota8, ltri=ltri,
            ones128=ones128, consts=consts,
            conv_w=np.ascontiguousarray(ii["L0_attn_b_conv_w"][:, lo:hi, 0, :]).astype(np.float32),
            conv_b=np.ascontiguousarray(ii["L0_attn_b_conv_b"][:, lo:hi])[:, None, :].astype(np.float32),
            dt_b=np.ascontiguousarray(ii["L0_attn_b_dt_b"][:, lo:hi])[:, None, :].astype(np.float32),
            a_log=np.ascontiguousarray(ii["L0_attn_b_A_log"][:, lo:hi]).astype(np.float32),
            d_p=np.ascontiguousarray(ii["L0_attn_b_D_p"][:, lo:hi])[:, None, :].astype(np.float32),
            norm_w=np.ascontiguousarray(ii["L0_attn_b_norm_w"])[:, None, :].astype(np.float32),
        )
        maps.append(m)
    return maps


def kernel(**inputs):
    maps = _host_in_maps(inputs)
    nc = _build()
    res = run_bass_kernel_spmd(nc, maps, core_ids=list(range(8)))
    out = np.empty((B, S, D), np.float32)
    for c in range(8):
        b, hf = divmod(c, 2)
        out[b, hf * TC:(hf + 1) * TC] = res.results[c]["out"]
    return out
